# version 04
# baseline (speedup 1.0000x reference)
import contextlib
import numpy as np

import concourse.bass as bass
import concourse.mybir as mybir
from concourse.bass_utils import run_bass_kernel_spmd
from concourse.alu_op_type import AluOpType as OP

F32 = mybir.dt.float32
AF = mybir.ActivationFunctionType
AX = mybir.AxisListType

ENGINES = ("tensor", "vector", "scalar", "gpsimd", "sync")

B, L, D = 2, 4096, 2048
H, HD = 8, 256
NCORE = 8
OWN = 1024
NQC = 4
KEXT = [8, 16, 24, 32]
NMASK = 8
LN_EPS = 1e-5
RMS_EPS = 1e-6

POOL = 50600
KB = 0
VB = 8192
QB = 16384
WKS = 16384
WVS = 20480
XF = 24576
XB = 32768
XSQ = 36864
XNT = 38912
KRW = 43008
KSQ = 44032
CKS = 45056
SKS = 45568
XNH = 32768
WQS = 40960
QRW = 45056
QS2 = 46080
CQF = 47104
SQF = 48128
CTX = 32768
PRB = 36864
MSK = 37376
XQT = 39424
XFO = 43520
RBS = 44032
OTB = 44288
WOS = 44800
HS = 0
XNQ = 16384
WGU = 20480
HAB = 28672
TMP = 29696
TMK = 30720
SGB = 30976
ZRO = 31488
WDS = 31744
RBT = 35840
OTF = 36352


class Prog:
    def __init__(self):
        self.ops = {e: [] for e in ENGINES}
        self.cnt = {}

    def emit(self, eng, opname, *args, waits=(), inc=None, incn=1, **kwargs):
        ws = [(s, v) for (s, v) in waits if v > 0]
        self.ops[eng].append((opname, args, kwargs, ws, inc, incn))
        if inc is not None:
            self.cnt[inc] = self.cnt.get(inc, 0) + incn
        return self.cnt.get(inc, 0)

    def c(self, sem):
        return self.cnt.get(sem, 0)

    def barrier(self):
        self.emit("gpsimd", "nop",
                  waits=[("sA", self.c("sA")), ("sG", self.c("sG"))], inc="bar")
        self.emit("sync", "nop", waits=[("sB", self.c("sB"))], inc="bar")
        self.emit("vector", "drain")
        self.emit("scalar", "drain")
        for e in ("tensor", "vector", "scalar"):
            self.emit(e, "nop", inc="bar")
        tgt = self.c("bar")
        for e in ENGINES:
            self.emit(e, "nop", waits=[("bar", tgt)])


def build_program(phases=4, dump=False):
    nc = bass.Bass("TRN2", target_bir_lowering=False, debug=False,
                   detect_race_conditions=False)

    d_xs = nc.dram_tensor("xs", [L, D], F32, kind="ExternalInput").ap()
    d_xq = nc.dram_tensor("xq", [OWN, D], F32, kind="ExternalInput").ap()
    d_ck = nc.dram_tensor("ckf", [128, L], F32, kind="ExternalInput").ap()
    d_sk = nc.dram_tensor("skf", [128, L], F32, kind="ExternalInput").ap()
    d_cq = nc.dram_tensor("cqf", [128, OWN], F32, kind="ExternalInput").ap()
    d_sq = nc.dram_tensor("sqf", [128, OWN], F32, kind="ExternalInput").ap()
    d_mask = nc.dram_tensor("mask", [NQC, 128, NMASK, 256], F32, kind="ExternalInput").ap()
    d_wq = nc.dram_tensor("wq", [16, 128, 16, 128], F32, kind="ExternalInput").ap()
    d_wk = nc.dram_tensor("wk", [128, 2, 16, 128], F32, kind="ExternalInput").ap()
    d_wv = nc.dram_tensor("wv", [128, 16, 256], F32, kind="ExternalInput").ap()
    d_wo = nc.dram_tensor("wo", [16, 128, 16, 128], F32, kind="ExternalInput").ap()
    d_wgu = nc.dram_tensor("wgu", [64, 128, 2, 16, 128], F32, kind="ExternalInput").ap()
    d_wd = nc.dram_tensor("wd", [16, 128, 64, 128], F32, kind="ExternalInput").ap()
    d_gq = nc.dram_tensor("gq", [128, 16], F32, kind="ExternalInput").ap()
    d_gk = nc.dram_tensor("gk", [128, 2], F32, kind="ExternalInput").ap()
    d_id = nc.dram_tensor("ident", [128, 128], F32, kind="ExternalInput").ap()
    d_out = nc.dram_tensor("out", [D, OWN], F32, kind="ExternalOutput").ap()
    d_xnsp = nc.dram_tensor("xnsp", [8, 128, D], F32)

    es = contextlib.ExitStack()
    F = es.enter_context(nc.sbuf_tensor("F", [128, POOL], F32)).ap()
    IDN = es.enter_context(nc.sbuf_tensor("IDN", [128, 128], F32)).ap()
    ONE = es.enter_context(nc.sbuf_tensor("ONE", [128, 1], F32)).ap()
    ONR = es.enter_context(nc.sbuf_tensor("ONR", [1, 128], F32)).ap()
    GQ = es.enter_context(nc.sbuf_tensor("GQ", [128, 16], F32)).ap()
    GK = es.enter_context(nc.sbuf_tensor("GK", [128, 2], F32)).ap()
    ST = es.enter_context(nc.sbuf_tensor("ST", [128, 16], F32)).ap()
    RW = es.enter_context(nc.sbuf_tensor("RW", [128, 2048], F32)).ap()
    PS = es.enter_context(nc.psum_tensor("PS", [128, 4096], F32)).ap()
    sems = {}
    for s in ("sA", "sB", "sC", "sD", "sE", "sG", "bar"):
        sems[s] = es.enter_context(nc.semaphore(s))

    p = Prog()

    def bank(b, n=512):
        return PS[:, b * 512:b * 512 + n]

    def bankrow(b, n=512):
        return PS[0:1, b * 512:b * 512 + n]

    def fsl(off, n):
        return F[:, off:off + n]

    def mm(out_ap, lhsT, rhs, start, stop, waits=(), inc=None):
        return p.emit("tensor", "matmul", out_ap, lhsT, rhs, start=start,
                      stop=stop, waits=waits, inc=inc)

    def vdrain():
        p.emit("vector", "drain")

    def rsqrt_cols(vc, wv):
        p.emit("scalar", "activation", ST[:, 6:7], ST[:, vc:vc + 1], AF.Sqrt,
               waits=wv, inc="sD")
        wd = [("sD", p.c("sD"))]
        p.emit("vector", "reciprocal", ST[:, 7:8], ST[:, 6:7], waits=wd)
        vdrain()
        p.emit("vector", "tensor_tensor", ST[:, 8:9], ST[:, 7:8], ST[:, 7:8], op=OP.mult)
        vdrain()
        p.emit("vector", "tensor_tensor", ST[:, 9:10], ST[:, 8:9], ST[:, vc:vc + 1], op=OP.mult)
        vdrain()
        p.emit("vector", "tensor_scalar", ST[:, 10:11], ST[:, 9:10], -0.5, 1.5,
               op0=OP.mult, op1=OP.add)
        vdrain()
        p.emit("vector", "tensor_tensor", ST[:, 11:12], ST[:, 7:8], ST[:, 10:11], op=OP.mult)

    def rsqrt_row(src_row, n, scale, eps, wout):
        veps = RW[0:1, 512:512 + n]
        Ar = RW[0:1, 1024:1024 + n]
        Br = RW[0:1, 1536:1536 + n]
        out = RW[0:1, 0:n]
        e = p.emit("vector", "tensor_scalar", veps, src_row, scale, eps,
                   op0=OP.mult, op1=OP.add, waits=[("sC", p.c("sC"))], inc="sE")
        p.emit("scalar", "activation", Ar, veps, AF.Sqrt,
               waits=[("sE", e)], inc="sD")
        p.emit("vector", "reciprocal", Br, Ar, waits=[("sD", p.c("sD"))])
        vdrain()
        p.emit("vector", "tensor_tensor", Ar, Br, Br, op=OP.mult)
        vdrain()
        p.emit("vector", "tensor_tensor", out, Ar, veps, op=OP.mult, waits=wout)
        vdrain()
        p.emit("vector", "tensor_scalar", Ar, out, -0.5, 1.5, op0=OP.mult, op1=OP.add)
        vdrain()
        p.emit("vector", "tensor_tensor", out, Br, Ar, op=OP.mult, inc="sE")
        return e

    p.emit("gpsimd", "dma_start", out=IDN[:, :], in_=d_id[:, :], inc="sA", incn=16)
    p.emit("gpsimd", "dma_start", out=GQ[:, :], in_=d_gq[:, :], inc="sA", incn=16)
    p.emit("gpsimd", "dma_start", out=GK[:, :], in_=d_gk[:, :], inc="sA", incn=16)
    p.emit("sync", "dma_start",
           out=fsl(WKS, 4096).rearrange("p (a b) -> p a b", a=32),
           in_=d_wk.rearrange("p fc dc f -> p (fc dc) f"), inc="sB", incn=16)
    p.emit("sync", "dma_start",
           out=fsl(WVS, 4096).rearrange("p (a b) -> p a b", a=16),
           in_=d_wv[:, :, :], inc="sB", incn=16)
    p.emit("vector", "memset", ONE[:, :], 1.0)
    p.emit("vector", "memset", ONR[:, :], 1.0, inc="sE")
    const_sE = p.c("sE")
    const_sA = p.c("sA")
    const_sB = p.c("sB")

    xslot_free = [("sE", 0), ("sE", 0)]
    xnt_free = [("sC", 0), ("sC", 0)]
    tbank_free = [("sD", 0), ("sD", 0)]
    kbank_free = [("sE", 0), ("sE", 0)]
    vbank_free = [("sD", 0), ("sD", 0)]
    cks_free = ("sE", 0)
    row4_free = ("sE", 0)
    krw_war = ("sD", 0)
    bc5_war = ("sC", 0)

    def xn_view(dcg, ci):
        v = fsl(XF, 8192).rearrange("p (dc t) -> p dc t", dc=16)
        return v[:, 4 * dcg:4 * dcg + 4, ci * 128:(ci + 1) * 128]

    def ln_chunk(src_ap, slot, ci):
        xb = fsl(XB + slot * 2048, 2048)
        xnt = fsl(XNT + slot * 2048, 2048)
        a = p.emit("gpsimd", "dma_start", out=xb, in_=src_ap,
                   waits=[xslot_free[slot], ("sA", const_sA)], inc="sA", incn=16)
        wA = [("sA", a)]
        p.emit("vector", "tensor_reduce", ST[:, 0:1], xb, axis=AX.X, op=OP.add,
               waits=wA)
        vdrain()
        p.emit("scalar", "activation", fsl(XSQ, 2048), xb, AF.Square,
               accum_out=ST[:, 1:2], waits=wA, inc="sD")
        wsq = [("sD", p.c("sD"))]
        p.emit("vector", "tensor_scalar", ST[:, 2:3], ST[:, 0:1], 1.0 / D, None,
               op0=OP.mult)
        p.emit("vector", "tensor_scalar", ST[:, 3:4], ST[:, 1:2], 1.0 / D, LN_EPS,
               op0=OP.mult, op1=OP.add, waits=wsq)
        vdrain()
        p.emit("vector", "tensor_tensor", ST[:, 4:5], ST[:, 2:3], ST[:, 2:3], op=OP.mult)
        vdrain()
        p.emit("vector", "tensor_tensor", ST[:, 5:6], ST[:, 3:4], ST[:, 4:5],
               op=OP.subtract, inc="sE")
        rsqrt_cols(5, [("sE", p.c("sE"))])
        vdrain()
        p.emit("vector", "tensor_tensor", ST[:, 12:13], ST[:, 2:3], ST[:, 11:12], op=OP.mult)
        vdrain()
        p.emit("vector", "tensor_scalar", ST[:, 13:14], ST[:, 12:13], -1.0, None,
               op0=OP.mult)
        p.emit("vector", "drain")
        e = p.emit("vector", "tensor_scalar", xnt, xb, ST[:, 11:12], ST[:, 13:14],
                   op0=OP.mult, op1=OP.add, waits=[xnt_free[slot]], inc="sE")
        xslot_free[slot] = ("sE", e)
        for g in range(4):
            tb = g % 2
            for d4 in range(4):
                w = []
                if d4 == 0:
                    w = [("sE", e), tbank_free[tb]]
                p.emit("tensor", "transpose", bank(tb)[:, d4 * 128:(d4 + 1) * 128],
                       xnt[:, (4 * g + d4) * 128:(4 * g + d4 + 1) * 128], IDN[:, :],
                       waits=w, inc="sC" if d4 == 3 else None)
            c = p.c("sC")
            dv = p.emit("scalar", "copy", xn_view(g, ci),
                        bank(tb).rearrange("p (a b) -> p a b", a=4),
                        waits=[("sC", c)], inc="sD")
            tbank_free[tb] = ("sD", dv)
            if g == 3:
                xnt_free[slot] = ("sC", c)

    for tile in range(8):
        for ci in range(4):
            ln_chunk(d_xs[(tile * 4 + ci) * 128:(tile * 4 + ci + 1) * 128, :],
                     (tile * 4 + ci) % 2, ci)
        xn_ready = p.c("sD")
        for fc in range(2):
            for dc in range(16):
                w = []
                if dc == 0:
                    w = [("sD", xn_ready), ("sB", const_sB), kbank_free[fc]]
                mm(bank(2 + fc), fsl(WKS + (fc * 16 + dc) * 128, 128),
                   fsl(XF + dc * 512, 512), dc == 0, dc == 15, waits=w,
                   inc="sC" if dc == 15 else None)
        kps = p.c("sC")
        for fc in range(2):
            p.emit("vector", "tensor_copy", fsl(KRW + fc * 512, 512), bank(2 + fc),
                   waits=[("sC", kps), krw_war] if fc == 0 else [], inc="sE")
        kraw_e = p.c("sE")
        kbank_free[0] = kbank_free[1] = ("sE", kraw_e)
        for fc in range(2):
            p.emit("scalar", "activation", fsl(KSQ + fc * 512, 512),
                   fsl(KRW + fc * 512, 512), AF.Square,
                   waits=[("sE", kraw_e)] if fc == 0 else [], inc="sD")
        ksq_d = p.c("sD")
        krw_war = ("sD", ksq_d)
        for fc in range(2):
            mm(bankrow(4), ONE[:, :], fsl(KSQ + fc * 512, 512), fc == 0, fc == 1,
               waits=[("sD", ksq_d), row4_free] if fc == 0 else [],
               inc="sC" if fc == 1 else None)
        ve = rsqrt_row(bankrow(4), 512, 1.0 / HD, RMS_EPS, [bc5_war])
        row4_free = ("sE", ve)
        rrow_e = p.c("sE")
        bc = mm(bank(5), ONR[:, :], RW[0:1, 0:512], True, True,
                waits=[("sE", rrow_e), ("sE", const_sE)], inc="sC")
        bc5_war = ("sC", bc)
        ca = p.emit("gpsimd", "dma_start", out=fsl(CKS, 512),
                    in_=d_ck[:, tile * 512:(tile + 1) * 512],
                    waits=[cks_free], inc="sA", incn=16)
        ca = p.emit("gpsimd", "dma_start", out=fsl(SKS, 512),
                    in_=d_sk[:, tile * 512:(tile + 1) * 512], inc="sA", incn=16)
        for fc in range(2):
            p.emit("vector", "tensor_tensor", fsl(KSQ + fc * 512, 512),
                   fsl(KRW + fc * 512, 512), bank(5), op=OP.mult,
                   waits=[("sC", bc)] if fc == 0 else [])
            p.emit("vector", "tensor_scalar", fsl(KRW + fc * 512, 512),
                   fsl(KSQ + fc * 512, 512), GK[:, fc:fc + 1], None, op0=OP.mult)
        k0 = fsl(KRW, 512); k1 = fsl(KRW + 512, 512)
        t0 = fsl(KSQ, 512); t1 = fsl(KSQ + 512, 512)
        ck = fsl(CKS, 512); sk = fsl(SKS, 512)
        p.emit("vector", "tensor_tensor", t0, k0, ck, op=OP.mult, waits=[("sA", ca)])
        p.emit("vector", "tensor_tensor", t1, k1, sk, op=OP.mult)
        p.emit("vector", "tensor_tensor", fsl(KB + tile * 512, 512), t0, t1,
               op=OP.subtract)
        p.emit("vector", "tensor_tensor", t0, k0, sk, op=OP.mult)
        e = p.emit("vector", "tensor_tensor", t1, k1, ck, op=OP.mult, inc="sE")
        cks_free = ("sE", e)
        p.emit("vector", "tensor_tensor", fsl(KB + 4096 + tile * 512, 512), t0, t1,
               op=OP.add, inc="sE")
        for ci in range(4):
            vb = 6 + ci % 2
            for dc in range(16):
                w = []
                if dc == 0:
                    w = [("sD", xn_ready), ("sB", const_sB), vbank_free[ci % 2]]
                mm(bank(vb, 256), fsl(XF + dc * 512 + ci * 128, 128),
                   fsl(WVS + dc * 256, 256), dc == 0, dc == 15, waits=w,
                   inc="sC" if dc == 15 else None)
            dv = p.emit("scalar", "copy",
                        fsl(VB + (tile * 4 + ci) * 256, 256), bank(vb, 256),
                        waits=[("sC", p.c("sC"))], inc="sD")
            vbank_free[ci % 2] = ("sD", dv)

    for tile in range(2):
        for ci in range(4):
            tc = tile * 4 + ci
            ln_chunk(d_xq[tc * 128:(tc + 1) * 128, :], tc % 2, ci)
        rd = p.c("sD")
        for ci in range(4):
            tc = tile * 4 + ci
            v = fsl(XF, 8192).rearrange("p (dc t) -> p dc t", dc=16)
            p.emit("gpsimd", "dma_start", out=d_xnsp[tc],
                   in_=v[:, :, ci * 128:(ci + 1) * 128],
                   waits=[("sD", rd)] if ci == 0 else [], inc="sG", incn=16)
        p.emit("tensor", "nop", waits=[("sG", p.c("sG"))])

    p.barrier()

    if phases < 2:
        return _finish(nc, es, p, sems, dump, F, d_xnsp)
    p.emit("gpsimd", "dma_start", out=fsl(CQF, 1024), in_=d_cq[:, :], inc="sA", incn=16)
    p.emit("gpsimd", "dma_start", out=fsl(SQF, 1024), in_=d_sq[:, :], inc="sA", incn=16)
    cq_a = p.c("sA")
    wq_free = [("bar", 0), ("bar", 0)]
    qbank_free = [("sE", 0), ("sE", 0)]
    row2_free = ("sE", 0)
    bc3_war = ("sC", 0)
    qrw_war = ("sD", 0)

    for half in range(2):
        xa = 0
        for k4 in range(4):
            v = fsl(XNH, 8192).rearrange("p (dc t) -> p dc t", dc=16)
            xa = p.emit("gpsimd", "dma_start", out=v[:, :, k4 * 128:(k4 + 1) * 128],
                        in_=d_xnsp[half * 4 + k4],
                        waits=[("sC", p.c("sC"))] if k4 == 0 else [],
                        inc="sA", incn=16)
        for h in range(H):
            bslots = []
            for i in range(2):
                b = p.emit("sync", "dma_start",
                           out=fsl(WQS + i * 2048, 2048).rearrange(
                               "p (dc f) -> p dc f", dc=16),
                           in_=d_wq[2 * h + i], waits=[wq_free[i]],
                           inc="sB", incn=16)
                bslots.append(b)
            for i in range(2):
                for dc in range(16):
                    w = []
                    if dc == 0:
                        w = [("sB", bslots[i]), ("sA", xa), qbank_free[i]]
                    mm(bank(i), fsl(WQS + i * 2048 + dc * 128, 128),
                       fsl(XNH + dc * 512, 512), dc == 0, dc == 15, waits=w,
                       inc="sC" if dc == 15 else None)
                wq_free[i] = ("sC", p.c("sC"))
            qmm = p.c("sC")
            for i in range(2):
                p.emit("vector", "tensor_copy", fsl(QRW + i * 512, 512), bank(i),
                       waits=[("sC", qmm), qrw_war] if i == 0 else [], inc="sE")
            qr = p.c("sE")
            qbank_free[0] = qbank_free[1] = ("sE", qr)
            for i in range(2):
                p.emit("scalar", "activation", fsl(QS2 + i * 512, 512),
                       fsl(QRW + i * 512, 512), AF.Square,
                       waits=[("sE", qr)] if i == 0 else [], inc="sD")
            qsqd = p.c("sD")
            qrw_war = ("sD", qsqd)
            for i in range(2):
                mm(bankrow(2), ONE[:, :], fsl(QS2 + i * 512, 512), i == 0, i == 1,
                   waits=[("sD", qsqd), row2_free] if i == 0 else [],
                   inc="sC" if i == 1 else None)
            ve = rsqrt_row(bankrow(2), 512, 1.0 / HD, RMS_EPS, [bc3_war])
            row2_free = ("sE", ve)
            bc = mm(bank(3), ONR[:, :], RW[0:1, 0:512], True, True,
                    waits=[("sE", p.c("sE"))], inc="sC")
            bc3_war = ("sC", bc)
            for i in range(2):
                fc = 2 * h + i
                p.emit("vector", "tensor_tensor", fsl(QS2 + i * 512, 512),
                       fsl(QRW + i * 512, 512), bank(3), op=OP.mult,
                       waits=[("sC", bc), ("sA", cq_a)] if i == 0 else [])
                p.emit("vector", "tensor_scalar", fsl(QRW + i * 512, 512),
                       fsl(QS2 + i * 512, 512), GQ[:, fc:fc + 1], None, op0=OP.mult)
            q0 = fsl(QRW, 512); q1 = fsl(QRW + 512, 512)
            t0 = fsl(QS2, 512); t1 = fsl(QS2 + 512, 512)
            cq = fsl(CQF + half * 512, 512); sq = fsl(SQF + half * 512, 512)
            o0 = fsl(QB + (2 * h) * 1024 + half * 512, 512)
            o1 = fsl(QB + (2 * h + 1) * 1024 + half * 512, 512)
            p.emit("vector", "tensor_tensor", t0, q0, cq, op=OP.mult)
            p.emit("vector", "tensor_tensor", t1, q1, sq, op=OP.mult)
            p.emit("vector", "tensor_tensor", o0, t0, t1, op=OP.subtract)
            p.emit("vector", "tensor_tensor", t0, q0, sq, op=OP.mult)
            p.emit("vector", "tensor_tensor", t1, q1, cq, op=OP.mult)
            p.emit("vector", "tensor_tensor", o1, t0, t1, op=OP.add, inc="sE")

    p.barrier()

    if phases < 3:
        return _finish(nc, es, p, sems, dump, F, d_xnsp)
    sbank_free = [("bar", 0), ("bar", 0)]
    probs_free = [("bar", 0), ("bar", 0)]
    cbank_free = ("bar", 0)
    row4_free = ("sE", 0)
    wo_free = [("bar", 0), ("bar", 0)]
    xfo_free = [("bar", 0), ("bar", 0)]
    otb_free = [("bar", 0), ("bar", 0)]
    tb6_free = ("bar", 0)
    obank_free = {5: ("bar", 0), 7: ("bar", 0)}
    b5_reader = ("sD", 0)

    for qc in range(NQC):
        kext = KEXT[qc]
        ma = p.emit("gpsimd", "dma_start",
                    out=fsl(MSK, 2048).rearrange("p (k q) -> p k q", k=8),
                    in_=d_mask[qc], waits=[("sE", p.c("sE"))], inc="sA", incn=16)
        xq_a = 0
        for i in range(2):
            xq_a = p.emit("gpsimd", "dma_start", out=fsl(XQT + i * 2048, 2048),
                          in_=d_xq[qc * 256 + i * 128: qc * 256 + (i + 1) * 128, :],
                          waits=[("sC", p.c("sC"))] if i == 0 else [],
                          inc="sA", incn=16)
        for h in range(H):
            for kc in range(kext):
                sb = kc % 2
                for fc in range(2):
                    w = []
                    if fc == 0:
                        w = [sbank_free[sb]]
                        if kc == 0:
                            w += [cbank_free, row4_free]
                    mm(bank(sb, 256), fsl(KB + fc * 4096 + kc * 128, 128),
                       fsl(QB + (2 * h + fc) * 1024 + qc * 256, 256),
                       fc == 0, fc == 1, waits=w, inc="sC" if fc == 1 else None)
                sc = p.c("sC")
                prb = fsl(PRB + sb * 256, 256)
                d = p.emit("scalar", "activation", prb, bank(sb, 256), AF.Exp,
                           scale=1.0 / 16.0,
                           waits=[("sC", sc), probs_free[sb]], inc="sD")
                sbank_free[sb] = ("sD", d)
                pw = ("sD", d)
                if kc >= kext - NMASK:
                    e = p.emit("vector", "tensor_tensor", prb, prb,
                               fsl(MSK + (kc - (kext - NMASK)) * 256, 256),
                               op=OP.mult, waits=[("sD", d), ("sA", ma)], inc="sE")
                    pw = ("sE", e)
                last = kc == kext - 1
                mm(bank(2, 256), fsl(VB + kc * 256, 128), prb, kc == 0, last,
                   waits=[pw])
                mm(bank(3, 256), fsl(VB + kc * 256 + 128, 128), prb, kc == 0, last)
                pe = mm(bankrow(4, 256), ONE[:, :], prb, kc == 0, last, inc="sC")
                probs_free[sb] = ("sC", pe)
            re = p.emit("vector", "reciprocal", RW[0:1, 0:256], bankrow(4, 256),
                        waits=[("sC", p.c("sC")), bc5_war], inc="sE")
            row4_free = ("sE", re)
            bc = mm(bank(5, 256), ONR[:, :], RW[0:1, 0:256], True, True,
                    waits=[("sE", re), obank_free[5]], inc="sC")
            bc5_war = ("sC", bc)
            d = p.emit("scalar", "copy", fsl(RBS, 256), bank(5, 256),
                       waits=[("sC", bc), cbank_free], inc="sD")
            b5_reader = ("sD", d)
            for i in range(2):
                p.emit("vector", "tensor_tensor",
                       fsl(CTX + (2 * h + i) * 256, 256), bank(2 + i, 256),
                       fsl(RBS, 256), op=OP.mult,
                       waits=[("sD", d)] if i == 0 else [], inc="sE")
            cbank_free = ("sE", p.c("sE"))
        ctx_e = p.c("sE")
        for oc in range(16):
            slot = oc % 2
            ob = 5 if oc % 2 == 0 else 7
            b = p.emit("sync", "dma_start",
                       out=fsl(WOS + slot * 2048, 2048).rearrange(
                           "p (fc o) -> p fc o", fc=16),
                       in_=d_wo[oc], waits=[wo_free[slot]], inc="sB", incn=16)
            for i in range(2):
                w = []
                if i == 0:
                    w = [("sA", xq_a), tb6_free]
                p.emit("tensor", "transpose", bank(6, 256)[:, i * 128:(i + 1) * 128],
                       fsl(XQT + i * 2048 + oc * 128, 128), IDN[:, :], waits=w,
                       inc="sC" if i == 1 else None)
            d = p.emit("scalar", "copy", fsl(XFO + slot * 256, 256), bank(6, 256),
                       waits=[("sC", p.c("sC")), xfo_free[slot]], inc="sD")
            tb6_free = ("sD", d)
            for fc in range(16):
                w = []
                if fc == 0:
                    w = [("sB", b), ("sE", ctx_e), obank_free[ob]]
                    if ob == 5:
                        w.append(b5_reader)
                mm(bank(ob, 256), fsl(WOS + slot * 2048 + fc * 128, 128),
                   fsl(CTX + fc * 256, 256), fc == 0, fc == 15, waits=w,
                   inc="sC" if fc == 15 else None)
            wo_free[slot] = ("sC", p.c("sC"))
            e = p.emit("vector", "tensor_tensor", fsl(OTB + slot * 256, 256),
                       bank(ob, 256), fsl(XFO + slot * 256, 256), op=OP.add,
                       waits=[("sC", p.c("sC")), ("sD", d), otb_free[slot]],
                       inc="sE")
            xfo_free[slot] = ("sE", e)
            obank_free[ob] = ("sE", e)
            g = p.emit("gpsimd", "dma_start",
                       out=d_out[oc * 128:(oc + 1) * 128, qc * 256:(qc + 1) * 256],
                       in_=fsl(OTB + slot * 256, 256), waits=[("sE", e)],
                       inc="sG", incn=16)
            otb_free[slot] = ("sG", g)

    p.barrier()

    if phases < 4:
        return _finish(nc, es, p, sems, dump, F, d_xnsp)
    wgu_free = [("bar", 0), ("bar", 0)]
    gbank_free = [("bar", 0), ("bar", 0)]
    ubank_free = [("bar", 0), ("bar", 0)]
    dbank_free = [("bar", 0), ("bar", 0)]
    sgb_free = [("bar", 0), ("bar", 0)]
    wd_free = [("bar", 0), ("bar", 0)]
    rbt_free = [("bar", 0), ("bar", 0)]
    otf_free = [("bar", 0), ("bar", 0)]
    store_before_p4 = p.c("sG")
    p.emit("vector", "memset", fsl(ZRO, 256), 0.0, inc="sE")

    for qt in range(4):
        xa = 0
        v = fsl(XNQ, 4096).rearrange("p (dc t) -> p dc t", dc=16)
        for k2 in range(2):
            xa = p.emit("gpsimd", "dma_start", out=v[:, :, k2 * 128:(k2 + 1) * 128],
                        in_=d_xnsp[qt * 2 + k2],
                        waits=[("sC", p.c("sC"))] if k2 == 0 else [],
                        inc="sA", incn=16)
        down_prev = ("sC", p.c("sC"))
        for q in range(64):
            slot = q % 2
            b = p.emit("sync", "dma_start",
                       out=fsl(WGU + slot * 4096, 4096).rearrange(
                           "p (gu dc f) -> p gu dc f", gu=2, dc=16),
                       in_=d_wgu[q], waits=[wgu_free[slot]], inc="sB", incn=16)
            gb, ub = q % 2, 2 + q % 2
            for dc in range(16):
                w = []
                if dc == 0:
                    w = [("sB", b), ("sA", xa), gbank_free[q % 2]]
                mm(bank(gb, 256), fsl(WGU + slot * 4096 + dc * 128, 128),
                   fsl(XNQ + dc * 256, 256), dc == 0, dc == 15, waits=w,
                   inc="sC" if dc == 15 else None)
            for dc in range(16):
                w = [ubank_free[q % 2]] if dc == 0 else []
                mm(bank(ub, 256), fsl(WGU + slot * 4096 + (16 + dc) * 128, 128),
                   fsl(XNQ + dc * 256, 256), dc == 0, dc == 15, waits=w,
                   inc="sC" if dc == 15 else None)
            wgu_free[slot] = ("sC", p.c("sC"))
            d = p.emit("scalar", "activation", fsl(SGB + (q % 2) * 256, 256),
                       bank(gb, 256), AF.Silu,
                       waits=[("sC", p.c("sC")), sgb_free[q % 2]], inc="sD")
            gbank_free[q % 2] = ("sD", d)
            hw = [("sD", d)]
            if q == 0:
                hw.append(down_prev)
            e = p.emit("vector", "tensor_tensor", fsl(HS + q * 256, 256),
                       fsl(SGB + (q % 2) * 256, 256), bank(ub, 256), op=OP.mult,
                       waits=hw, inc="sE")
            ubank_free[q % 2] = ("sE", e)
            sgb_free[q % 2] = ("sE", e)
            if q % 4 == 3:
                hq = [fsl(HS + (q - 3 + t) * 256, 256) for t in range(4)]
                ha = [fsl(HAB + t * 256, 256) for t in range(4)]
                tp = [fsl(TMP + t * 256, 256) for t in range(4)]
                he = p.c("sE")
                for t in range(4):
                    p.emit("scalar", "activation", ha[t], hq[t], AF.Abs,
                           waits=[("sE", he)] if t == 0 else [],
                           inc="sD" if t == 3 else None)
                p.emit("vector", "tensor_tensor", tp[0], ha[0], ha[1], op=OP.max,
                       waits=[("sD", p.c("sD"))])
                p.emit("vector", "tensor_tensor", tp[1], ha[2], ha[3], op=OP.max)
                p.emit("vector", "tensor_tensor", tp[2], tp[0], tp[1], op=OP.min)
                p.emit("vector", "tensor_tensor", tp[0], ha[0], ha[1], op=OP.min)
                p.emit("vector", "tensor_tensor", tp[1], ha[2], ha[3], op=OP.min)
                p.emit("vector", "tensor_tensor", tp[3], tp[0], tp[1], op=OP.max)
                p.emit("vector", "tensor_tensor", tp[0], tp[2], tp[3], op=OP.max)
                for t in range(4):
                    p.emit("vector", "tensor_tensor", fsl(TMK, 256), ha[t], tp[0],
                           op=OP.is_ge)
                    p.emit("vector", "tensor_tensor", hq[t], hq[t], fsl(TMK, 256),
                           op=OP.mult, inc="sE" if t == 3 else None)
        hs_done = p.c("sE")
        for hc in range(16):
            dbk = 4 + hc % 2
            for qg in range(4):
                slot = qg % 2
                b = p.emit("sync", "dma_start",
                           out=fsl(WDS + slot * 2048, 2048).rearrange(
                               "p (q f) -> p q f", q=16),
                           in_=d_wd[hc, :, qg * 16:(qg + 1) * 16, :],
                           waits=[wd_free[slot]], inc="sB", incn=16)
                for qi in range(16):
                    q = qg * 16 + qi
                    w = []
                    if qi == 0:
                        w = [("sB", b), ("sE", hs_done)]
                        if qg == 0:
                            w.append(dbank_free[hc % 2])
                    mm(bank(dbk, 256), fsl(WDS + slot * 2048 + qi * 128, 128),
                       fsl(HS + q * 256, 256), q == 0, q == 63, waits=w,
                       inc="sC" if qi == 15 else None)
                wd_free[slot] = ("sC", p.c("sC"))
            dmm = p.c("sC")
            slot = hc % 2
            g = p.emit("gpsimd", "dma_start", out=fsl(RBT + slot * 256, 256),
                       in_=d_out[hc * 128:(hc + 1) * 128, qt * 256:(qt + 1) * 256],
                       waits=[("sG", store_before_p4), rbt_free[slot]],
                       inc="sA", incn=16)
            e = p.emit("vector", "tensor_tensor", fsl(OTF + slot * 256, 256),
                       bank(dbk, 256), fsl(RBT + slot * 256, 256), op=OP.add,
                       waits=[("sC", dmm), ("sA", g), otf_free[slot]], inc="sE")
            dbank_free[hc % 2] = ("sE", e)
            rbt_free[slot] = ("sE", e)
            g2 = p.emit("gpsimd", "dma_start",
                        out=d_out[hc * 128:(hc + 1) * 128, qt * 256:(qt + 1) * 256],
                        in_=fsl(OTF + slot * 256, 256), waits=[("sE", e)],
                        inc="sG", incn=16)
            otf_free[slot] = ("sG", g2)

    p.barrier()
    return _finish(nc, es, p, sems, dump, F, d_xnsp)


def _finish(nc, es, p, sems, dump=False, F=None, d_xnsp=None):
    if dump:
        d_dbg = nc.dram_tensor("dbgF", [128, POOL], F32, kind="ExternalOutput").ap()
        d_dbg2 = nc.dram_tensor("dbgsp", [8, 128, D], F32, kind="ExternalOutput").ap()
        dbgb = es.enter_context(nc.sbuf_tensor("dbgb", [128, 256], F32)).ap()
        nch = (POOL + 4095) // 4096
        for i in range(nch):
            n = min(4096, POOL - i * 4096)
            p.emit("gpsimd", "dma_start", out=d_dbg[:, i*4096:i*4096+n],
                   in_=F[:, i*4096:i*4096+n], inc="sG", incn=16)
        for tc in range(8):
            for i in range(8):
                p.emit("gpsimd", "dma_start", out=dbgb[:, :],
                       in_=d_xnsp[tc][:, i*256:(i+1)*256],
                       waits=[("sG", p.c("sG"))], inc="sA", incn=16)
                p.emit("gpsimd", "dma_start", out=d_dbg2[tc][:, i*256:(i+1)*256],
                       in_=dbgb[:, :], waits=[("sA", p.c("sA"))], inc="sG", incn=16)
        p.emit("gpsimd", "nop", waits=[("sG", p.c("sG"))])
    with es, nc.Block() as block:
        def mk(engname):
            def body(eng):
                for op, args, kwargs, ws, inc, incn in p.ops[engname]:
                    for (sn, val) in ws:
                        eng.wait_ge(sems[sn], val)
                    inst = getattr(eng, op)(*args, **kwargs)
                    if inc is not None:
                        inst.then_inc(sems[inc], incn)
            return body
        block.tensor(mk("tensor"))
        block.vector(mk("vector"))
        block.scalar(mk("scalar"))
        block.gpsimd(mk("gpsimd"))
        block.sync(mk("sync"))
    return nc



def _core_inputs(c, x, shared, ckf_h, skf_h):
    seq, j = c // 4, c % 4
    chunks = [j, 7 - j, 8 + j, 15 - j]
    own_idx = np.concatenate([np.arange(ch * 256, (ch + 1) * 256) for ch in chunks])
    xs = np.ascontiguousarray(x[seq])
    xq = np.ascontiguousarray(x[seq][own_idx])
    cqf = np.ascontiguousarray(ckf_h[:, own_idx])
    sqf = np.ascontiguousarray(skf_h[:, own_idx])
    mask = np.zeros((NQC, 128, NMASK, 256), np.float32)
    for qc in range(NQC):
        base = KEXT[qc] * 128 - 1024
        kpos = base + np.arange(1024)
        qpos = own_idx[qc * 256:(qc + 1) * 256]
        m = (kpos[:, None] <= qpos[None, :]).astype(np.float32)
        mask[qc] = m.reshape(NMASK, 128, 256).transpose(1, 0, 2)
    d = {"xs": xs, "xq": xq, "ckf": ckf_h, "skf": skf_h,
         "cqf": cqf, "sqf": sqf, "mask": mask}
    d.update(shared)
    return d, own_idx


def kernel(**inputs):
    x = np.asarray(inputs["L0_attn_x"], np.float32)
    q_proj = np.asarray(inputs["L0_attn_q_proj"], np.float32)
    k_proj = np.asarray(inputs["L0_attn_k_proj"], np.float32)
    v_proj = np.asarray(inputs["L0_attn_v_proj"], np.float32)
    o_proj = np.asarray(inputs["L0_attn_o_proj"], np.float32)
    q_norm = np.asarray(inputs["L0_attn_q_norm"], np.float32)
    k_norm = np.asarray(inputs["L0_attn_k_norm"], np.float32)
    w_gate = np.asarray(inputs["L0_ffn_w_gate"], np.float32)
    w_up = np.asarray(inputs["L0_ffn_w_up"], np.float32)
    w_down = np.asarray(inputs["L0_ffn_w_down"], np.float32)

    wq_h = np.ascontiguousarray(q_proj.reshape(16, 128, 16, 128).transpose(0, 3, 2, 1))
    wk_h = np.ascontiguousarray(k_proj.reshape(2, 128, 16, 128).transpose(3, 0, 2, 1))
    wv_h = np.ascontiguousarray(v_proj.reshape(256, 16, 128).transpose(2, 1, 0))
    wo_h = np.ascontiguousarray(o_proj.reshape(16, 128, 16, 128).transpose(0, 3, 2, 1))
    wg_t = w_gate.reshape(16, 128, 128, 64).transpose(3, 1, 0, 2)
    wu_t = w_up.reshape(16, 128, 128, 64).transpose(3, 1, 0, 2)
    wgu_h = np.ascontiguousarray(np.stack([wg_t, wu_t], axis=2))
    wd_h = np.ascontiguousarray(w_down.reshape(128, 64, 16, 128).transpose(2, 0, 1, 3))
    gq_h = np.ascontiguousarray(np.tile(q_norm, 8).reshape(16, 128).T)
    gk_h = np.ascontiguousarray(k_norm.reshape(2, 128).T)

    idx = np.arange(128, dtype=np.float64)
    inv = 1.0 / (1e6 ** (2.0 * idx / 256.0))
    pos = np.arange(L, dtype=np.float64)
    fr = np.outer(inv, pos)
    ckf_h = np.cos(fr).astype(np.float32)
    skf_h = np.sin(fr).astype(np.float32)

    shared = {"wq": wq_h, "wk": wk_h, "wv": wv_h, "wo": wo_h, "wgu": wgu_h,
              "wd": wd_h, "gq": gq_h, "gk": gk_h,
              "ident": np.eye(128, dtype=np.float32)}

    in_maps, own_idxs = [], []
    for c in range(NCORE):
        m, oi = _core_inputs(c, x, shared, ckf_h, skf_h)
        in_maps.append(m)
        own_idxs.append(oi)

    nc = build_program()
    res = run_bass_kernel_spmd(nc, in_maps, core_ids=list(range(NCORE)))
    out = np.empty((B, L, D), np.float32)
    for c in range(NCORE):
        out[c // 4, own_idxs[c]] = np.asarray(res.results[c]["out"]).T
    return out
